# version 01
# baseline (speedup 1.0000x reference)
import math
import contextlib
import numpy as np
import ml_dtypes
import concourse.bass as bass
import concourse.mybir as mybir
from concourse.bass_utils import run_bass_kernel_spmd

F32 = mybir.dt.float32
BF16 = mybir.dt.bfloat16
I32 = mybir.dt.int32
AF = mybir.ActivationFunctionType
ALU = mybir.AluOpType
AX = mybir.AxisListType

NCORES = 8
B, S, D = 4, 2048, 2048
NH, DH = 16, 128
INTER = 5504
OCN = INTER // 128
DI, DTR, DS, DCV = 4096, 128, 16, 4
NE, ME, TOPK = 32, 768, 8
RT = 1024
NT = B * S
SCALE = 1.0 / math.sqrt(float(DH))
BF = ml_dtypes.bfloat16

PHYS = ("sync", "tensor", "vector", "scalar", "gpsimd")
COMPUTE = ("tensor", "vector", "scalar")
QUEUE = {"sync": "sync", "tq": "scalar", "gq": "gpsimd"}
SEMS = ("sync", "tq", "gq", "tensor", "vector", "scalar", "gpsimd")


class Prog:

    def __init__(self, name):
        self.nc = bass.Bass("TRN2", target_bir_lowering=False)
        self.name = name
        self.ops = {e: [] for e in PHYS}
        self.cnt = {s: 0 for s in SEMS}
        self.waited = {e: {s: 0 for s in SEMS} for e in PHYS}

    def din(self, name, shape, dtype):
        return self.nc.dram_tensor(name, list(shape), dtype, kind="ExternalInput").ap()

    def dout(self, name, shape, dtype):
        return self.nc.dram_tensor(name, list(shape), dtype, kind="ExternalOutput").ap()

    def dtmp(self, name, shape, dtype):
        return self.nc.dram_tensor(name, list(shape), dtype, kind="Internal").ap()

    def _emit(self, phys, sem, fn, inc, waits):
        w = []
        for (ws, wv) in (waits or []):
            if ws != sem and wv > self.waited[phys].get(ws, 0):
                w.append((ws, wv))
                self.waited[phys][ws] = wv
        if phys in COMPUTE:
            for o in COMPUTE:
                if o != phys and self.cnt[o] > self.waited[phys][o]:
                    w.append((o, self.cnt[o]))
                    self.waited[phys][o] = self.cnt[o]
        if sem in ("vector", "scalar", "gpsimd") and self.cnt[sem] > 0:
            w.append((sem, self.cnt[sem]))
        self.ops[phys].append((fn, sem, inc, w))
        self.cnt[sem] += inc
        return (sem, self.cnt[sem])

    def tok(self, sem):
        return (sem, self.cnt[sem])

    def alltok(self):
        return [self.tok(e) for e in COMPUTE] + [self.tok("gq")]

    def dma(self, dst, src, q="sync", waits=()):
        return self._emit(QUEUE[q], q, lambda e: e.dma_start(dst, src), 16, waits)

    def mm(self, out, lhsT, rhs, start, stop, waits=()):
        n = rhs.shape[-1]
        if n <= 512:
            return self._emit(
                "tensor", "tensor",
                lambda e: e.matmul(out, lhsT, rhs, start=start, stop=stop,
                                   skip_group_check=True), 1, waits)
        t = None
        for i in range(0, n, 512):
            j = min(i + 512, n)
            o_sl = out[:, i:j]
            r_sl = rhs[:, i:j]
            t = self._emit(
                "tensor", "tensor",
                lambda e, o=o_sl, r=r_sl: e.matmul(
                    o, lhsT, r, start=start, stop=stop,
                    skip_group_check=True), 1, waits if i == 0 else ())
        return t

    def tr(self, out_psum, in_sb, ident, waits=()):
        return self._emit("tensor", "tensor",
                          lambda e: e.transpose(out_psum, in_sb, ident), 1, waits)

    def init_consts(self, tile, values):
        self.cst = tile
        self.cmap = {}
        for i, v in enumerate(values):
            self.v("memset", tile[:, i:i + 1], float(v))
            self.cmap[float(v)] = i

    def c(self, v):
        i = self.cmap[float(v)]
        return self.cst[:, i:i + 1]

    def act(self, out, in_, func, bias=0.0, scale=1.0, waits=()):
        if isinstance(bias, (int, float)) and float(bias) != 0.0:
            bias = self.c(bias)
        if isinstance(scale, (int, float)) and float(scale) != 1.0:
            scale = self.c(scale)
        return self._emit(
            "scalar", "scalar",
            lambda e: e.activation(out, in_, func, bias=bias, scale=scale), 1, waits)

    def scopy(self, out, in_, waits=()):
        return self._emit("scalar", "scalar", lambda e: e.copy(out, in_), 1, waits)

    def sadd(self, out, in_, add, waits=()):
        return self._emit("scalar", "scalar", lambda e: e.add(out, in_, add), 1, waits)

    def v(self, method, *args, waits=(), eng="vector", **kwargs):
        return self._emit(eng, eng,
                          lambda e: getattr(e, method)(*args, **kwargs), 1, waits)

    def final_wait(self, tokens, eng="scalar"):
        self.ops[eng].append((None, None, 0, list(tokens)))

    def build(self):
        nc = self.nc
        with contextlib.ExitStack() as stk:
            sems = {s: stk.enter_context(nc.semaphore(f"sem_{s}")) for s in SEMS}
            blk = stk.enter_context(nc.Block())
            for ename in PHYS:
                stream = self.ops[ename]
                if not stream:
                    continue

                def run(eng, stream=stream):
                    pending = []
                    for fn, sem, inc, waits in stream:
                        pending.extend(waits)
                        if fn is None:
                            continue
                        seen = {}
                        for (ws, wv) in pending:
                            seen[ws] = max(seen.get(ws, 0), wv)
                        for ws, wv in seen.items():
                            eng.wait_ge(sems[ws], wv)
                        pending = []
                        inst = fn(eng)
                        if inc:
                            inst.then_inc(sems[sem], inc)
                    seen = {}
                    for (ws, wv) in pending:
                        seen[ws] = max(seen.get(ws, 0), wv)
                    for ws, wv in seen.items():
                        eng.wait_ge(sems[ws], wv)
                getattr(blk, ename)(run)
        return nc


_uid = [0]


def _un(name):
    _uid[0] += 1
    return f"{name}_{_uid[0]}"


def sbt(stk, nc, name, shape, dtype):
    return stk.enter_context(nc.sbuf_tensor(_un(name), list(shape), dtype))


def pst(stk, nc, name, shape, dtype=F32):
    return stk.enter_context(nc.psum_tensor(_un(name), list(shape), dtype))


def col_view(t, c0, w, p=128):
    return t.rearrange("(a p) c -> p a c", p=p)[:, :, c0:c0 + w]


IDENT_F32 = np.eye(128, dtype=np.float32)
IDENT_BF16 = np.eye(128, dtype=np.float32).astype(BF)

_cache = {}


def get_nc(name, builder):
    if name not in _cache:
        _cache[name] = builder()
    return _cache[name]


def ln_tile(p, x_sb, xn_out, tmp_sq, red, waits=(), n=D):
    p.v("reduce_sum", red[:, 0:1], x_sb[:, :], AX.X, waits=list(waits))
    p.act(tmp_sq[:, :], x_sb[:, :], AF.Square)
    p.v("reduce_sum", red[:, 1:2], tmp_sq[:, :], AX.X)
    p.v("tensor_scalar_mul", red[:, 2:3], red[:, 0:1], 1.0 / n)
    p.v("tensor_scalar_mul", red[:, 3:4], red[:, 1:2], 1.0 / n)
    p.v("tensor_mul", red[:, 4:5], red[:, 2:3], red[:, 2:3])
    p.v("tensor_sub", red[:, 3:4], red[:, 3:4], red[:, 4:5])
    p.act(red[:, 5:6], red[:, 3:4], AF.Ln, bias=1e-5)
    p.act(red[:, 6:7], red[:, 5:6], AF.Exp, scale=-0.5)
    return p.v("tensor_scalar", xn_out[:, :], x_sb[:, :], red[:, 2:3], red[:, 6:7],
               ALU.subtract, ALU.mult)


def build_ln0():
    p = Prog("ln0")
    nc = p.nc
    xr = p.din("xr", [RT, D], F32)
    identb = p.din("identb", [128, 128], BF16)
    xn_r = p.dout("xn_r", [RT, D], BF16)
    xnT_c = p.dout("xnT_c", [D, RT], BF16)
    with contextlib.ExitStack() as st:
        x_sb = sbt(st, nc, "x_sb", [128, D], F32)
        sq = sbt(st, nc, "sq", [128, D], F32)
        red = sbt(st, nc, "red", [128, 8], F32)
        xn_sb = sbt(st, nc, "xn_sb", [128, D], BF16)
        cst = sbt(st, nc, "cst", [128, 4], F32)
        p.init_consts(cst, [1e-5, -0.5])
        id_sb = sbt(st, nc, "id_sb", [128, 128], BF16)
        tp = pst(st, nc, "tp", [128, 128], BF16)
        tcol = sbt(st, nc, "tcol", [128, 128], BF16)
        tid = p.dma(id_sb[:, :], identb[:, :])
        last = []
        for tt in range(RT // 128):
            tl = p.dma(x_sb[:, :], xr[tt * 128:(tt + 1) * 128, :],
                       waits=p.alltok())
            tn = ln_tile(p, x_sb, xn_sb, sq, red, waits=[tl, p.tok("gq")])
            so = p.dma(xn_r[tt * 128:(tt + 1) * 128, :], xn_sb[:, :],
                       q="gq", waits=[tn])
            for dc in range(D // 128):
                ttr = p.tr(tp[:, :], xn_sb[:, dc * 128:(dc + 1) * 128], id_sb[:, :],
                           waits=[tn, tid])
                tc = p.scopy(tcol[:, :], tp[:, :], waits=[ttr, p.tok("gq")])
                so2 = p.dma(xnT_c[dc * 128:(dc + 1) * 128, tt * 128:(tt + 1) * 128],
                            tcol[:, :], q="gq", waits=[tc])
            last = [so, so2]
        p.final_wait(last, eng="scalar")
    return p.build()


def build_decode():
    p = Prog("decode")
    nc = p.nc
    g_idx = p.din("g_idx", [6 * 128, D], I32)
    u_idx = p.din("u_idx", [6 * 128, D], I32)
    d_idx = p.din("d_idx", [2 * 128, INTER], I32)
    lut = p.din("lut", [16], F32)
    identf = p.din("identf", [128, 128], F32)
    gT_part = p.dout("gT_part", [D, 6 * 128], BF16)
    uT_part = p.dout("uT_part", [D, 6 * 128], BF16)
    dT_part = p.dout("dT_part", [INTER, 2 * 128], BF16)
    with contextlib.ExitStack() as st:
        cst = sbt(st, nc, "cst", [128, 20], F32)
        p.init_consts(cst, [float(-k) for k in range(1, 16)] + [-1.0, 1.0])
        idx_i = sbt(st, nc, "idx_i", [128, INTER], I32)
        idx_f = sbt(st, nc, "idx_f", [128, INTER], F32)
        acc = sbt(st, nc, "acc", [128, INTER], F32)
        tmp1 = sbt(st, nc, "tmp1", [128, INTER], F32)
        lut_row = sbt(st, nc, "lut_row", [1, 16], F32)
        lutb = sbt(st, nc, "lutb", [128, 16], F32)
        ones = sbt(st, nc, "ones", [1, 128], F32)
        id_sb = sbt(st, nc, "id_sb", [128, 128], F32)
        tp = pst(st, nc, "tp", [128, 128], F32)
        lp = pst(st, nc, "lp", [128, 16], F32)
        tcol = sbt(st, nc, "tcol", [128, 128], BF16)
        tid = p.dma(id_sb[:, :], identf[:, :])
        tlu = p.dma(lut_row[:, :], lut.rearrange("(o k) -> o k", o=1))
        ton = p.v("memset", ones[:, :], 1.0)
        tm = p.mm(lp[:, :], ones[:, :], lut_row[:, :], True, True,
                  waits=[tlu, ton])
        p.scopy(lutb[:, :], lp[:, :], waits=[tm])
        so = None

        def strip(idx_dram, r0, ncols, out_dram, oc0):
            tl = p.dma(idx_i[:, :ncols], idx_dram[r0:r0 + 128, :ncols],
                       waits=p.alltok())
            p.v("tensor_copy", idx_f[:, :ncols], idx_i[:, :ncols], waits=[tl])
            for k in range(16):
                p.act(tmp1[:, :ncols], idx_f[:, :ncols], AF.Square, bias=-float(k))
                p.act(tmp1[:, :ncols], tmp1[:, :ncols], AF.Relu, scale=-1.0,
                      bias=1.0)
                if k == 0:
                    p.v("tensor_scalar_mul", acc[:, :ncols], tmp1[:, :ncols],
                        lutb[:, 0:1])
                else:
                    p.v("scalar_tensor_tensor", acc[:, :ncols], tmp1[:, :ncols],
                        lutb[:, k:k + 1], acc[:, :ncols], ALU.mult, ALU.add)
            s = None
            for cc in range(ncols // 128):
                ttr = p.tr(tp[:, :], acc[:, cc * 128:(cc + 1) * 128], id_sb[:, :],
                           waits=[p.tok("vector"), tid])
                tc = p.scopy(tcol[:, :], tp[:, :], waits=[ttr, p.tok("gq")])
                s = p.dma(out_dram[cc * 128:(cc + 1) * 128,
                                   oc0 * 128:(oc0 + 1) * 128],
                          tcol[:, :], q="gq", waits=[tc])
            return s

        for j in range(6):
            so = strip(g_idx, j * 128, D, gT_part, j)
        for j in range(6):
            so = strip(u_idx, j * 128, D, uT_part, j)
        for j in range(2):
            so = strip(d_idx, j * 128, INTER, dT_part, j)
        p.final_wait([so], eng="scalar")
    return p.build()


def build_attn0():
    p = Prog("attn0")
    nc = p.nc
    xnT = p.din("xnT", [D, S], BF16)
    xnTq = p.din("xnTq", [D, RT], BF16)
    wq = p.din("wq", [NH, D, DH], F32)
    wk = p.din("wk", [NH, D, DH], F32)
    wv = p.din("wv", [NH, D, DH], F32)
    wo = p.din("wo", [NH, DH, D], F32)
    bqkv = p.din("bqkv", [16, 3 * 128], F32)
    bo = p.din("bo", [D], F32)
    mqk = p.din("mqk", [128, 16], F32)
    identf = p.din("identf", [128, 128], F32)
    identb = p.din("identb", [128, 128], BF16)
    attn_r = p.dout("attn_r", [RT, D], F32)
    NDC = D // 128

    with contextlib.ExitStack() as st:
        onesf = sbt(st, nc, "onesf", [1, 128], F32)
        onesb = sbt(st, nc, "onesb", [128, 1], BF16)
        cst = sbt(st, nc, "cst", [128, 4], F32)
        p.init_consts(cst, [SCALE])
        bT = sbt(st, nc, "bT", [128, 3 * 16], F32)
        bob = sbt(st, nc, "bob", [128, D], BF16)
        z_all = sbt(st, nc, "z_all", [128, NH * RT], BF16)
        psA = pst(st, nc, "psA", [128, RT], F32)
        psB = pst(st, nc, "psB", [128, RT], F32)
        psC = pst(st, nc, "psC", [128, RT], F32)
        p.v("memset", onesf[:, :], 1.0)
        p.v("memset", onesb[:, :], 1.0)

        with contextlib.ExitStack() as sti:
            id_f = sbt(sti, nc, "id_f", [128, 128], F32)
            id_b = sbt(sti, nc, "id_b", [128, 128], BF16)
            biases = sbt(sti, nc, "biases", [16, 3 * 128], F32)
            bo_row = sbt(sti, nc, "bo_row", [1, D], F32)
            mqk_sb = sbt(sti, nc, "mqk_sb", [128, 16], F32)
            iota0 = sbt(sti, nc, "iota0", [128, RT], F32)
            mask = sbt(sti, nc, "mask", [128, RT], BF16)
            xnt = sbt(sti, nc, "xnt", [128, NDC * S], BF16)
            xntq = sbt(sti, nc, "xntq", [128, NDC * RT], BF16)
            q_h = sbt(sti, nc, "q_h", [128, RT], BF16)
            wstage = sbt(sti, nc, "wstage", [128, 2048], F32)
            w_bf = sbt(sti, nc, "w_bf", [128, 3 * 2048], BF16)
            k_sb = sbt(sti, nc, "k_sb", [128, S], BF16)
            v_sb = sbt(sti, nc, "v_sb", [128, S], BF16)
            at_sb = sbt(sti, nc, "at_sb", [128, RT], BF16)
            den_sb = sbt(sti, nc, "den_sb", [1, RT], F32)
            rb_sb = sbt(sti, nc, "rb_sb", [128, RT], F32)

            tid = p.dma(id_f[:, :], identf[:, :])
            tib = p.dma(id_b[:, :], identb[:, :])
            tmq = p.dma(mqk_sb[:, :], mqk[:, :])
            tbi = p.dma(biases[:, :], bqkv[:, :])
            tbo = p.dma(bo_row[:, :], bo.rearrange("(o d) -> o d", o=1))
            tio = p.v("iota", iota0[:, :], [[1, RT]], base=0,
                      channel_multiplier=-1, eng="gpsimd",
                      allow_small_or_imprecise_dtypes=True)
            with contextlib.ExitStack() as st2:
                ps16 = pst(st2, nc, "ps16", [128, 16], F32)
                pbo = pst(st2, nc, "pbo", [128, 512], F32)
                for j in range(3):
                    ttr = p.tr(ps16[:, :], biases[0:16, j * 128:(j + 1) * 128],
                               id_f[0:16, 0:16], waits=[tbi, tid])
                    p.scopy(bT[:, j * 16:(j + 1) * 16], ps16[:, :], waits=[ttr])
                for j in range(4):
                    tb = p.mm(pbo[:, :], onesf[:, :],
                              bo_row[:, j * 512:(j + 1) * 512], True, True,
                              waits=[tbo])
                    p.scopy(bob[:, j * 512:(j + 1) * 512], pbo[:, :], waits=[tb])
            tx = {}
            for dc in range(NDC):
                tx[("f", dc)] = p.dma(xnt[:, dc * S:(dc + 1) * S],
                                      xnT[dc * 128:(dc + 1) * 128, :])
                tx[("q", dc)] = p.dma(xntq[:, dc * RT:(dc + 1) * RT],
                                      xnTq[dc * 128:(dc + 1) * 128, :])

            for h in range(NH):
                wtoks = []
                for j, w in enumerate((wq, wk, wv)):
                    tw = p.dma(wstage[:, :].rearrange("p (a c) -> p a c", c=128),
                               col_view(w[h, :, :], 0, 128),
                               q="tq", waits=p.alltok())
                    wtoks.append(p.v("tensor_copy",
                                     w_bf[:, j * 2048:(j + 1) * 2048],
                                     wstage[:, :], waits=[tw]))

                def wtile(j, dc):
                    return w_bf[:, j * 2048 + dc * 128:j * 2048 + (dc + 1) * 128]

                for dc in range(NDC):
                    p.mm(psA[:, :], wtile(0, dc),
                         xntq[:, dc * RT:(dc + 1) * RT], dc == 0, dc == NDC - 1,
                         waits=[wtoks[0], tx[("q", dc)]])
                p.sadd(q_h[:, :], psA[:, :], bT[:, h:h + 1],
                       waits=[p.tok("tensor")])
                for half in range(2):
                    for dc in range(NDC):
                        p.mm(psB[:, :], wtile(1, dc),
                             xnt[:, dc * S + half * RT:dc * S + (half + 1) * RT],
                             dc == 0, dc == NDC - 1,
                             waits=[wtoks[1], tx[("f", dc)]])
                    p.sadd(k_sb[:, half * RT:(half + 1) * RT], psB[:, :],
                           bT[:, 16 + h:17 + h], waits=[p.tok("tensor")])
                    for dc in range(NDC):
                        p.mm(psB[:, :], wtile(2, dc),
                             xnt[:, dc * S + half * RT:dc * S + (half + 1) * RT],
                             dc == 0, dc == NDC - 1, waits=[wtoks[2]])
                    p.sadd(v_sb[:, half * RT:(half + 1) * RT], psB[:, :],
                           bT[:, 32 + h:33 + h], waits=[p.tok("tensor")])
                with contextlib.ExitStack() as st3:
                    ps_t = pst(st3, nc, "ps_t", [128, 128], BF16)
                    for kc in range(16):
                        ttr = p.tr(ps_t[:, :], v_sb[:, kc * 128:(kc + 1) * 128],
                                   id_b[:, :], waits=[tib])
                        p.scopy(v_sb[:, kc * 128:(kc + 1) * 128], ps_t[:, :],
                                waits=[ttr])
                with contextlib.ExitStack() as st4:
                    psD1 = pst(st4, nc, "psD1", [1, 512], F32)
                    psD2 = pst(st4, nc, "psD2", [1, 512], F32)
                    for kc in range(16):
                        ts = p.mm(psA[:, :], k_sb[:, kc * 128:(kc + 1) * 128],
                                  q_h[:, :], True, True)
                        p.act(at_sb[:, :], psA[:, :], AF.Exp, scale=SCALE,
                              waits=[ts])
                        p.v("tensor_single_scalar", mask[:, :], iota0[:, :],
                            mqk_sb[:, kc:kc + 1], ALU.is_ge, waits=[tmq, tio])
                        tmu = p.v("tensor_mul", at_sb[:, :], at_sb[:, :],
                                  mask[:, :])
                        p.mm(psD1[:, :], onesb[:, :], at_sb[:, 0:512],
                             kc == 0, kc == 15, waits=[tmu])
                        p.mm(psD2[:, :], onesb[:, :], at_sb[:, 512:1024],
                             kc == 0, kc == 15)
                        p.mm(psC[:, :], v_sb[:, kc * 128:(kc + 1) * 128],
                             at_sb[:, :], kc == 0, kc == 15)
                    tz = p.tok("tensor")
                    p.scopy(den_sb[:, 0:512], psD1[:, :], waits=[tz])
                    p.scopy(den_sb[:, 512:1024], psD2[:, :])
                    p.v("reciprocal", den_sb[:, :], den_sb[:, :])
                for j in range(2):
                    p.mm(psA[:, j * 512:(j + 1) * 512], onesf[:, :],
                         den_sb[:, j * 512:(j + 1) * 512], True, True,
                         waits=[p.tok("vector")])
                p.scopy(rb_sb[:, :], psA[:, :], waits=[p.tok("tensor")])
                p.v("tensor_mul", z_all[:, h * RT:(h + 1) * RT], psC[:, :],
                    rb_sb[:, :])
        with contextlib.ExitStack() as sto:
            acc = sbt(sto, nc, "acc", [128, 8 * D], F32)
            wost = sbt(sto, nc, "wost", [128, D], F32)
            wobf = sbt(sto, nc, "wobf", [128, D], BF16)
            orow = sbt(sto, nc, "orow", [128, D], F32)
            for h in range(NH):
                two = p.dma(wost[:, :], wo[h, :, :], q="tq", waits=p.alltok())
                tcw = p.v("tensor_copy", wobf[:, :], wost[:, :], waits=[two])
                for tt in range(8):
                    for j in range(2):
                        tmm = p.mm(psA[:, :],
                                   z_all[:, h * RT + tt * 128:
                                         h * RT + (tt + 1) * 128],
                                   wobf[:, j * 1024:(j + 1) * 1024],
                                   True, True, waits=[tcw])
                        dst = acc[:, tt * D + j * 1024: tt * D + (j + 1) * 1024]
                        if h == 0:
                            p.scopy(dst, psA[:, :], waits=[tmm])
                        else:
                            p.v("tensor_add", dst, dst, psA[:, :], waits=[tmm])
            so = None
            for tt in range(8):
                tv = p.v("tensor_add", orow[:, :], acc[:, tt * D:(tt + 1) * D],
                         bob[:, :], waits=[p.tok("gq")])
                so = p.dma(attn_r[tt * 128:(tt + 1) * 128, :], orow[:, :],
                           q="gq", waits=[tv])
            p.final_wait([so], eng="scalar")
    return p.build()


def build_ffn0():
    p = Prog("ffn0")
    nc = p.nc
    xnTq = p.din("xnTq", [D, RT], BF16)
    gT = p.din("gT", [D, INTER], BF16)
    uT = p.din("uT", [D, INTER], BF16)
    dT = p.din("dT", [INTER, D], BF16)
    ffnT = p.dout("ffnT", [D, RT], F32)
    NDC = D // 128
    W = NDC * 128
    WD = OCN * 128
    with contextlib.ExitStack() as st:
        xntq = sbt(st, nc, "xntq", [128, NDC * RT], BF16)
        gt = sbt(st, nc, "gt", [128, 2 * W], BF16)
        ut = sbt(st, nc, "ut", [128, 2 * W], BF16)
        dt = sbt(st, nc, "dt", [128, 2 * WD], BF16)
        h_all = sbt(st, nc, "h_all", [128, OCN * RT], BF16)
        sg = sbt(st, nc, "sg", [128, RT], F32)
        orow = sbt(st, nc, "orow", [128, RT], F32)
        pg = pst(st, nc, "pg", [128, RT], F32)
        pu = pst(st, nc, "pu", [128, RT], F32)
        pd = pst(st, nc, "pd", [128, RT], F32)
        tx = {}
        for dc in range(NDC):
            tx[dc] = p.dma(xntq[:, dc * RT:(dc + 1) * RT],
                           xnTq[dc * 128:(dc + 1) * 128, :])
        consumed = [[], []]
        for oc in range(OCN):
            sl = oc % 2
            tg = p.dma(gt[:, sl * W:(sl + 1) * W]
                       .rearrange("p (a c) -> p a c", c=128),
                       col_view(gT, oc * 128, 128), q="sync",
                       waits=consumed[sl])
            tu = p.dma(ut[:, sl * W:(sl + 1) * W]
                       .rearrange("p (a c) -> p a c", c=128),
                       col_view(uT, oc * 128, 128), q="tq",
                       waits=consumed[sl])
            for dc in range(NDC):
                p.mm(pg[:, :], gt[:, sl * W + dc * 128:sl * W + (dc + 1) * 128],
                     xntq[:, dc * RT:(dc + 1) * RT], dc == 0, dc == NDC - 1,
                     waits=[tg, tx[dc]])
            for dc in range(NDC):
                p.mm(pu[:, :], ut[:, sl * W + dc * 128:sl * W + (dc + 1) * 128],
                     xntq[:, dc * RT:(dc + 1) * RT], dc == 0, dc == NDC - 1,
                     waits=[tu])
            tm = p.tok("tensor")
            p.act(sg[:, :], pg[:, :], AF.Silu, waits=[tm])
            p.v("tensor_mul", h_all[:, oc * RT:(oc + 1) * RT], sg[:, :], pu[:, :])
            consumed[sl] = [p.tok("tensor"), p.tok("vector")]
        so = None
        dcons = [[], []]
        for dc in range(NDC):
            sl = dc % 2
            td = p.dma(dt[:, sl * WD:(sl + 1) * WD]
                       .rearrange("p (a c) -> p a c", c=128),
                       col_view(dT, dc * 128, 128), q="sync", waits=dcons[sl])
            for oc in range(OCN):
                p.mm(pd[:, :], dt[:, sl * WD + oc * 128:sl * WD + (oc + 1) * 128],
                     h_all[:, oc * RT:(oc + 1) * RT], oc == 0, oc == OCN - 1,
                     waits=[td])
            dcons[sl] = [p.tok("tensor")]
            tc = p.scopy(orow[:, :], pd[:, :],
                         waits=[p.tok("tensor"), p.tok("gq")])
            so = p.dma(ffnT[dc * 128:(dc + 1) * 128, :], orow[:, :], q="gq",
                       waits=[tc])
        p.final_wait([so], eng="scalar")
    return p.build()


def build_resln1():
    p = Prog("resln1")
    nc = p.nc
    xr = p.din("xr", [RT, D], F32)
    ar = p.din("ar", [RT, D], F32)
    ffnT = p.din("ffnT", [D, RT], F32)
    gk = p.din("gk", [D, NE], F32)
    identf = p.din("identf", [128, 128], F32)
    identb = p.din("identb", [128, 128], BF16)
    x1_r = p.dout("x1_r", [RT, D], F32)
    xn1T_c = p.dout("xn1T_c", [D, RT], BF16)
    wnT = p.dout("wnT", [NE, RT], F32)
    NDC = D // 128
    with contextlib.ExitStack() as st:
        id_f = sbt(st, nc, "id_f", [128, 128], F32)
        id_b = sbt(st, nc, "id_b", [128, 128], BF16)
        cst = sbt(st, nc, "cst", [128, 4], F32)
        p.init_consts(cst, [1e-5, -0.5])
        frow = sbt(st, nc, "frow", [128, 8 * D], F32)
        fcol = sbt(st, nc, "fcol", [128, RT], F32)
        x_sb = sbt(st, nc, "x_sb", [128, D], F32)
        a_sb = sbt(st, nc, "a_sb", [128, D], F32)
        sq = sbt(st, nc, "sq", [128, D], F32)
        red = sbt(st, nc, "red", [128, 8], F32)
        xn_sb = sbt(st, nc, "xn_sb", [128, D], BF16)
        tcol = sbt(st, nc, "tcol", [128, 128], BF16)
        gkst = sbt(st, nc, "gkst", [128, NDC * NE], F32)
        gkbf = sbt(st, nc, "gkbf", [128, NDC * NE], BF16)
        prob = sbt(st, nc, "prob", [128, NE], F32)
        pcur = sbt(st, nc, "pcur", [128, NE], F32)
        sel = sbt(st, nc, "sel", [128, NE], F32)
        t2 = sbt(st, nc, "t2", [128, NE], F32)
        mx = sbt(st, nc, "mx", [128, 2], F32)
        wn = sbt(st, nc, "wn", [128, NE], F32)
        tpf = pst(st, nc, "tpf", [128, 128], F32)
        tpb = pst(st, nc, "tpb", [128, 128], BF16)
        pl = pst(st, nc, "pl", [128, NE], F32)
        pw = pst(st, nc, "pw", [NE, 128], F32)
        tid = p.dma(id_f[:, :], identf[:, :])
        tib = p.dma(id_b[:, :], identb[:, :])
        tgk = p.dma(gkst[:, :].rearrange("p (a c) -> p a c", c=NE),
                    col_view(gk, 0, NE))
        p.v("tensor_copy", gkbf[:, :], gkst[:, :], waits=[tgk])
        for dc in range(NDC):
            tf = p.dma(fcol[:, :], ffnT[dc * 128:(dc + 1) * 128, :],
                       waits=[p.tok("tensor"), p.tok("scalar")])
            for tt in range(8):
                ttr = p.tr(tpf[:, :], fcol[:, tt * 128:(tt + 1) * 128],
                           id_f[:, :], waits=[tf, tid])
                p.scopy(frow[:, tt * D + dc * 128:tt * D + (dc + 1) * 128],
                        tpf[:, :], waits=[ttr])
        last = []
        for tt in range(8):
            tlx = p.dma(x_sb[:, :], xr[tt * 128:(tt + 1) * 128, :],
                        waits=p.alltok())
            tla = p.dma(a_sb[:, :], ar[tt * 128:(tt + 1) * 128, :], q="tq",
                        waits=p.alltok())
            p.v("tensor_add", x_sb[:, :], x_sb[:, :], a_sb[:, :],
                waits=[tlx, tla])
            p.v("tensor_add", x_sb[:, :], x_sb[:, :],
                frow[:, tt * D:(tt + 1) * D])
            so = p.dma(x1_r[tt * 128:(tt + 1) * 128, :], x_sb[:, :], q="gq",
                       waits=[p.tok("vector")])
            tn = ln_tile(p, x_sb, xn_sb, sq, red, waits=[p.tok("gq")])
            for dc in range(NDC):
                ttr = p.tr(tpb[:, :], xn_sb[:, dc * 128:(dc + 1) * 128],
                           id_b[:, :], waits=[tn, tib])
                tc = p.scopy(tcol[:, :], tpb[:, :], waits=[ttr, p.tok("gq")])
                so2 = p.dma(xn1T_c[dc * 128:(dc + 1) * 128,
                                   tt * 128:(tt + 1) * 128],
                            tcol[:, :], q="gq", waits=[tc])
                p.mm(pl[:, :], tcol[:, :], gkbf[:, dc * NE:(dc + 1) * NE],
                     dc == 0, dc == NDC - 1, waits=[tc])
            tm = p.tok("tensor")
            p.v("reduce_max", mx[:, 0:1], pl[:, :], AX.X, waits=[tm])
            p.v("tensor_scalar_mul", mx[:, 1:2], mx[:, 0:1], -1.0)
            p.act(prob[:, :], pl[:, :], AF.Exp, bias=mx[:, 1:2])
            p.v("reduce_sum", mx[:, 0:1], prob[:, :], AX.X)
            p.v("reciprocal", mx[:, 0:1], mx[:, 0:1])
            p.v("tensor_scalar_mul", prob[:, :], prob[:, :], mx[:, 0:1])
            p.v("tensor_copy", pcur[:, :], prob[:, :])
            p.v("memset", sel[:, :], 0.0)
            for it in range(TOPK):
                p.v("reduce_max", mx[:, 0:1], pcur[:, :], AX.X)
                p.v("tensor_single_scalar", t2[:, :], pcur[:, :], mx[:, 0:1],
                    ALU.is_ge)
                p.v("tensor_add", sel[:, :], sel[:, :], t2[:, :])
                p.v("tensor_mul", t2[:, :], t2[:, :], pcur[:, :])
                p.v("tensor_sub", pcur[:, :], pcur[:, :], t2[:, :])
            p.v("tensor_mul", wn[:, :], prob[:, :], sel[:, :])
            p.v("reduce_sum", mx[:, 0:1], wn[:, :], AX.X)
            p.v("reciprocal", mx[:, 0:1], mx[:, 0:1])
            tws = p.v("tensor_scalar_mul", wn[:, :], wn[:, :], mx[:, 0:1])
            ttr = p.tr(pw[:, :], wn[:, :], id_f[:, :], waits=[tws])
            tc2 = p.scopy(fcol[0:NE, 0:128], pw[:, :], waits=[ttr, p.tok("gq")])
            so3 = p.dma(wnT[:, tt * 128:(tt + 1) * 128], fcol[0:NE, 0:128],
                        q="gq", waits=[tc2])
            last = [so, so2, so3]
        p.final_wait(last, eng="scalar")
    return p.build()


def build_mamba1():
    p = Prog("mamba1")
    nc = p.nc
    xn1T = p.din("xn1T", [D, S], BF16)
    ipwTx = p.din("ipwTx", [D, DI], F32)
    ipwTz = p.din("ipwTz", [D, 2048], F32)
    xpwT = p.din("xpwT", [DI, 160], F32)
    cw = p.din("cw", [DI, DCV], F32)
    cb = p.din("cb", [32, 128], F32)
    identf = p.din("identf", [128, 128], F32)
    xc_all = p.dout("xc_all", [DI, S], BF16)
    z_l = p.dout("z_l", [2048, S], BF16)
    dtlo = p.dout("dtlo", [128, S], BF16)
    BCo = p.dout("BCo", [32, S], BF16)
    NDC = D // 128
    NCH = DI // 128
    W = NDC * 128
    with contextlib.ExitStack() as st:
        xnt = sbt(st, nc, "xnt", [128, NDC * S], BF16)
        wst = sbt(st, nc, "wst", [128, 2 * W], F32)
        wbf = sbt(st, nc, "wbf", [128, 2 * W], BF16)
        xpst = sbt(st, nc, "xpst", [128, 160], F32)
        xpbf = sbt(st, nc, "xpbf", [128, NCH * 160], BF16)
        cwsb = sbt(st, nc, "cwsb", [128, NCH * DCV], F32)
        cbrows = sbt(st, nc, "cbrows", [32, 128], F32)
        cbT = sbt(st, nc, "cbT", [128, 32], F32)
        id_f = sbt(st, nc, "id_f", [128, 128], F32)
        xb = sbt(st, nc, "xb", [128, RT], F32)
        halo = sbt(st, nc, "halo", [128, NCH * 3], F32)
        accv = sbt(st, nc, "accv", [128, RT], F32)
        xcbf = sbt(st, nc, "xcbf", [128, RT], BF16)
        zbf = sbt(st, nc, "zbf", [128, RT], BF16)
        dt_sb = sbt(st, nc, "dt_sb", [128, RT], BF16)
        bc_sb = sbt(st, nc, "bc_sb", [32, RT], BF16)
        pxb = pst(st, nc, "pxb", [128, RT], F32)
        pdt = pst(st, nc, "pdt", [128, RT], F32)
        pbc = pst(st, nc, "pbc", [32, RT], F32)
        p16 = pst(st, nc, "p16", [128, 32], F32)

        tid = p.dma(id_f[:, :], identf[:, :])
        tcb = p.dma(cbrows[:, :], cb[:, :])
        ttr = p.tr(p16[:, :], cbrows[:, :], id_f[0:32, 0:32],
                   waits=[tcb, tid])
        p.scopy(cbT[:, :], p16[:, :], waits=[ttr])
        tcw = p.dma(cwsb[:, :].rearrange("p (a c) -> p a c", c=DCV),
                    cw.rearrange("(a p) c -> p a c", p=128))
        for cc in range(NCH):
            tw = p.dma(xpst[:, :], xpwT[cc * 128:(cc + 1) * 128, :],
                       waits=p.alltok())
            p.v("tensor_copy", xpbf[:, cc * 160:(cc + 1) * 160], xpst[:, :],
                waits=[tw])
        txs = {}
        for dc in range(NDC):
            txs[dc] = p.dma(xnt[:, dc * S:(dc + 1) * S],
                            xn1T[dc * 128:(dc + 1) * 128, :])
        last = []
        for th in range(2):
            for cc in range(NCH):
                sl = cc % 2
                tw = p.dma(wst[:, sl * W:(sl + 1) * W]
                           .rearrange("p (a c) -> p a c", c=128),
                           col_view(ipwTx, cc * 128, 128), q="tq",
                           waits=p.alltok())
                tc = p.v("tensor_copy", wbf[:, sl * W:(sl + 1) * W],
                         wst[:, sl * W:(sl + 1) * W], waits=[tw])
                for dc in range(NDC):
                    p.mm(pxb[:, :],
                         wbf[:, sl * W + dc * 128:sl * W + (dc + 1) * 128],
                         xnt[:, dc * S + th * RT:dc * S + (th + 1) * RT],
                         dc == 0, dc == NDC - 1, waits=[tc, txs[dc]])
                p.scopy(xb[:, :], pxb[:, :], waits=[p.tok("tensor")])
                p.v("tensor_scalar_mul", accv[:, :], xb[:, :],
                    cwsb[:, cc * DCV + 3:cc * DCV + 4], waits=[tcw])
                for sh in range(1, 4):
                    jj = 3 - sh
                    p.v("scalar_tensor_tensor", accv[:, sh:RT],
                        xb[:, 0:RT - sh],
                        cwsb[:, cc * DCV + jj:cc * DCV + jj + 1],
                        accv[:, sh:RT], ALU.mult, ALU.add)
                if th == 1:
                    for jj in range(3):
                        wdt = 3 - jj
                        p.v("scalar_tensor_tensor", accv[:, 0:wdt],
                            halo[:, cc * 3 + jj:cc * 3 + 3],
                            cwsb[:, cc * DCV + jj:cc * DCV + jj + 1],
                            accv[:, 0:wdt], ALU.mult, ALU.add)
                else:
                    p.v("tensor_copy", halo[:, cc * 3:cc * 3 + 3],
                        xb[:, RT - 3:RT])
                tsl = p.act(xcbf[:, :], accv[:, :], AF.Silu,
                            bias=cbT[:, cc:cc + 1], waits=[p.tok("gq")])
                p.mm(pdt[:, :], xpbf[:, cc * 160:cc * 160 + 128], xcbf[:, :],
                     cc == 0, cc == NCH - 1, waits=[tsl])
                p.mm(pbc[:, :], xpbf[:, cc * 160 + 128:(cc + 1) * 160],
                     xcbf[:, :], cc == 0, cc == NCH - 1)
                last.append(p.dma(xc_all[cc * 128:(cc + 1) * 128,
                                         th * RT:(th + 1) * RT],
                                  xcbf[:, :], q="gq", waits=[tsl]))
            tdt = p.scopy(dt_sb[:, :], pdt[:, :],
                          waits=[p.tok("tensor"), p.tok("gq")])
            tbc = p.scopy(bc_sb[:, :], pbc[:, :])
            last.append(p.dma(dtlo[:, th * RT:(th + 1) * RT], dt_sb[:, :],
                              q="gq", waits=[tdt]))
            last.append(p.dma(BCo[:, th * RT:(th + 1) * RT], bc_sb[:, :],
                              q="gq", waits=[tbc]))
            for zc in range(16):
                sl = zc % 2
                tw = p.dma(wst[:, sl * W:(sl + 1) * W]
                           .rearrange("p (a c) -> p a c", c=128),
                           col_view(ipwTz, zc * 128, 128), q="tq",
                           waits=p.alltok())
                tc = p.v("tensor_copy", wbf[:, sl * W:(sl + 1) * W],
                         wst[:, sl * W:(sl + 1) * W], waits=[tw])
                for dc in range(NDC):
                    p.mm(pxb[:, :],
                         wbf[:, sl * W + dc * 128:sl * W + (dc + 1) * 128],
                         xnt[:, dc * S + th * RT:dc * S + (th + 1) * RT],
                         dc == 0, dc == NDC - 1, waits=[tc])
                tz = p.scopy(zbf[:, :], pxb[:, :],
                             waits=[p.tok("tensor"), p.tok("gq")])
                last.append(p.dma(z_l[zc * 128:(zc + 1) * 128,
                                      th * RT:(th + 1) * RT],
                                  zbf[:, :], q="gq", waits=[tz]))
        p.final_wait(last[-3:], eng="scalar")
    return p.build()


def build_scan():
    p = Prog("scan")
    nc = p.nc
    dtlo = p.din("dtlo", [128, S], BF16)
    BCi = p.din("BCi", [32, S], BF16)
    xc = p.din("xc", [2048, S], BF16)
    zi = p.din("zi", [2048, S], BF16)
    dtwT = p.din("dtwT", [DTR, 2048], F32)
    dtb = p.din("dtb", [16, 128], F32)
    alog = p.din("alog", [2048, DS], F32)
    dvec = p.din("dvec", [16, 128], F32)
    opwT = p.din("opwT", [2048, D], F32)
    identf = p.din("identf", [128, 128], F32)
    mpart = p.dout("mpart", [D, S], F32)
    TC = 512
    NCHK = S // TC
    NG = 16
    with contextlib.ExitStack() as st:
        id_f = sbt(st, nc, "id_f", [128, 128], F32)
        cst = sbt(st, nc, "cst", [128, 4], F32)
        p.init_consts(cst, [1.0])
        dtlo_sb = sbt(st, nc, "dtlo_sb", [128, S], BF16)
        dtw_st = sbt(st, nc, "dtw_st", [128, 2048], F32)
        dtw_bf = sbt(st, nc, "dtw_bf", [128, 2048], BF16)
        dtbT = sbt(st, nc, "dtbT", [128, 16], F32)
        dT_sb = sbt(st, nc, "dT_sb", [128, 16], F32)
        rows16 = sbt(st, nc, "rows16", [16, 128], F32)
        negA = sbt(st, nc, "negA", [128, NG * DS], F32)
        B_sb = sbt(st, nc, "B_sb", [1, 16 * TC], BF16)
        C_sb = sbt(st, nc, "C_sb", [1, 16 * TC], BF16)
        onesb = sbt(st, nc, "onesb", [1, 128], BF16)
        Bbc = sbt(st, nc, "Bbc", [128, 16 * TC], BF16)
        Cbc = sbt(st, nc, "Cbc", [128, 16 * TC], BF16)
        delta = sbt(st, nc, "delta", [128, TC], F32)
        dx = sbt(st, nc, "dx", [128, TC], BF16)
        xcg = sbt(st, nc, "xcg", [128, NG * TC], BF16)
        zg = sbt(st, nc, "zg", [128, TC], BF16)
        zs = sbt(st, nc, "zs", [128, TC], F32)
        dA = sbt(st, nc, "dA", [128, TC], BF16)
        du = sbt(st, nc, "du", [128, TC], BF16)
        hbuf = sbt(st, nc, "hbuf", [128, TC], BF16)
        tmpv = sbt(st, nc, "tmpv", [128, TC], BF16)
        yacc = sbt(st, nc, "yacc", [128, TC], F32)
        ybf = sbt(st, nc, "ybf", [128, NG * TC], BF16)
        carry = sbt(st, nc, "carry", [128, NG * DS], F32)
        opst = sbt(st, nc, "opst", [128, 2048], F32)
        opbf = sbt(st, nc, "opbf", [128, 2048], BF16)
        mrow = sbt(st, nc, "mrow", [128, TC], F32)
        etmp = sbt(st, nc, "etmp", [128, TC], F32)
        pdl = pst(st, nc, "pdl", [128, TC], F32)
        pbb = pst(st, nc, "pbb", [128, TC], F32)
        pop = pst(st, nc, "pop", [128, TC], F32)
        p16b = pst(st, nc, "p16b", [128, 16], F32)

        tid = p.dma(id_f[:, :], identf[:, :])
        tdl = p.dma(dtlo_sb[:, :], dtlo[:, :])
        tw = p.dma(dtw_st[:, :], dtwT[:, :])
        p.v("tensor_copy", dtw_bf[:, :], dtw_st[:, :], waits=[tw])
        p.v("memset", onesb[:, :], 1.0)
        p.v("memset", carry[:, :], 0.0)
        tb1 = p.dma(rows16[:, :], dtb[:, :], waits=p.alltok())
        ttr = p.tr(p16b[:, :], rows16[:, :], id_f[0:16, 0:16], waits=[tb1, tid])
        p.scopy(dtbT[:, :], p16b[:, :], waits=[ttr])
        tb2 = p.dma(rows16[:, :], dvec[:, :], waits=p.alltok())
        ttr = p.tr(p16b[:, :], rows16[:, :], id_f[0:16, 0:16], waits=[tb2])
        p.scopy(dT_sb[:, :], p16b[:, :], waits=[ttr])
        for g in range(NG):
            ta = p.dma(dtw_st[:, 0:DS], alog[g * 128:(g + 1) * 128, :],
                       waits=p.alltok())
            p.act(negA[:, g * DS:(g + 1) * DS], dtw_st[:, 0:DS], AF.Exp,
                  waits=[ta])
            p.v("tensor_scalar_mul", negA[:, g * DS:(g + 1) * DS],
                negA[:, g * DS:(g + 1) * DS], -1.0)
        last = []
        for ch in range(NCHK):
            t0 = ch * TC
            bc_toks = []
            for n in range(16):
                bc_toks.append(p.dma(B_sb[:, n * TC:(n + 1) * TC],
                                     BCi[n:n + 1, t0:t0 + TC], q="tq",
                                     waits=[p.tok("tensor")]))
                bc_toks.append(p.dma(C_sb[:, n * TC:(n + 1) * TC],
                                     BCi[16 + n:17 + n, t0:t0 + TC], q="tq"))
            for n in range(16):
                tm1 = p.mm(pbb[:, :], onesb[:, :],
                           B_sb[:, n * TC:(n + 1) * TC], True, True,
                           waits=bc_toks)
                p.scopy(Bbc[:, n * TC:(n + 1) * TC], pbb[:, :], waits=[tm1])
                tm2 = p.mm(pbb[:, :], onesb[:, :],
                           C_sb[:, n * TC:(n + 1) * TC], True, True)
                p.scopy(Cbc[:, n * TC:(n + 1) * TC], pbb[:, :], waits=[tm2])
            txc = {}
            for g in range(NG):
                txc[g] = p.dma(xcg[:, g * TC:(g + 1) * TC],
                               xc[g * 128:(g + 1) * 128, t0:t0 + TC],
                               waits=p.alltok())
            for g in range(NG):
                tdm = p.mm(pdl[:, :], dtw_bf[:, g * 128:(g + 1) * 128],
                           dtlo_sb[:, t0:t0 + TC], True, True, waits=[tdl])
                p.act(etmp[:, :], pdl[:, :], AF.Exp, bias=dtbT[:, g:g + 1],
                      waits=[tdm])
                p.act(delta[:, :], etmp[:, :], AF.Ln, bias=1.0)
                p.v("tensor_mul", dx[:, :], delta[:, :],
                    xcg[:, g * TC:(g + 1) * TC], waits=[txc[g]])
                for n in range(16):
                    p.act(dA[:, :], delta[:, :], AF.Exp,
                          scale=negA[:, g * DS + n:g * DS + n + 1])
                    p.v("tensor_mul", du[:, :], dx[:, :],
                        Bbc[:, n * TC:(n + 1) * TC])
                    p.v("tensor_tensor_scan", hbuf[:, :], dA[:, :], du[:, :],
                        carry[:, g * DS + n:g * DS + n + 1], ALU.mult, ALU.add)
                    p.v("tensor_copy", carry[:, g * DS + n:g * DS + n + 1],
                        hbuf[:, TC - 1:TC])
                    p.v("tensor_mul", tmpv[:, :], hbuf[:, :],
                        Cbc[:, n * TC:(n + 1) * TC])
                    if n == 0:
                        p.v("tensor_copy", yacc[:, :], tmpv[:, :])
                    else:
                        p.v("tensor_add", yacc[:, :], yacc[:, :], tmpv[:, :])
                p.v("scalar_tensor_tensor", yacc[:, :],
                    xcg[:, g * TC:(g + 1) * TC], dT_sb[:, g:g + 1], yacc[:, :],
                    ALU.mult, ALU.add)
                tz = p.dma(zg[:, :], zi[g * 128:(g + 1) * 128, t0:t0 + TC],
                           waits=p.alltok())
                p.act(zs[:, :], zg[:, :], AF.Silu, waits=[tz])
                p.v("tensor_mul", ybf[:, g * TC:(g + 1) * TC], yacc[:, :],
                    zs[:, :])
            for dc in range(16):
                two = p.dma(opst[:, :].rearrange("p (a c) -> p a c", c=128),
                            col_view(opwT, dc * 128, 128), q="tq",
                            waits=p.alltok())
                tcw = p.v("tensor_copy", opbf[:, :], opst[:, :], waits=[two])
                for g in range(NG):
                    p.mm(pop[:, :], opbf[:, g * 128:(g + 1) * 128],
                         ybf[:, g * TC:(g + 1) * TC], g == 0, g == NG - 1,
                         waits=[tcw])
                tc = p.scopy(mrow[:, :], pop[:, :],
                             waits=[p.tok("tensor"), p.tok("gq")])
                last.append(p.dma(mpart[dc * 128:(dc + 1) * 128, t0:t0 + TC],
                                  mrow[:, :], q="gq", waits=[tc]))
        p.final_wait(last[-2:], eng="scalar")
    return p.build()


def build_moe():
    p = Prog("moe")
    nc = p.nc
    xn1T = p.din("xn1T", [D, NT], BF16)
    gp = p.din("gp", [4, D, ME], F32)
    up = p.din("up", [4, D, ME], F32)
    dp = p.din("dp", [4, ME, D], F32)
    wnl = p.din("wnl", [4, NT], F32)
    moe_part = p.dout("moe_part", [D, NT], BF16)
    scr = p.dtmp("scr", [4, D, NT], BF16)
    NDC = D // 128
    NMC = ME // 128
    TG = 512
    NTG = NT // TG
    XW = NDC * TG
    with contextlib.ExitStack() as st:
        wn_sb = sbt(st, nc, "wn_sb", [1, NT], F32)
        onesf = sbt(st, nc, "onesf", [1, 128], F32)
        gst = sbt(st, nc, "gst", [128, NDC * 128], F32)
        gbf = sbt(st, nc, "gbf", [128, NMC * NDC * 128], BF16)
        ubf = sbt(st, nc, "ubf", [128, NMC * NDC * 128], BF16)
        dbf = sbt(st, nc, "dbf", [128, NMC * NDC * 128], BF16)
        xnb = sbt(st, nc, "xnb", [128, 2 * XW], BF16)
        hw = sbt(st, nc, "hw", [128, NMC * TG], BF16)
        sg = sbt(st, nc, "sg", [128, TG], F32)
        wnb = sbt(st, nc, "wnb", [128, TG], F32)
        prow = sbt(st, nc, "prow", [128, TG], BF16)
        arow = sbt(st, nc, "arow", [128, 2 * TG], BF16)
        pg = pst(st, nc, "pg", [128, TG], F32)
        pu = pst(st, nc, "pu", [128, TG], F32)
        pd = pst(st, nc, "pd", [128, TG], F32)
        pw = pst(st, nc, "pw", [128, TG], F32)
        p.v("memset", onesf[:, :], 1.0)
        last = []
        xcons = [[], []]
        for e in range(4):
            twn = p.dma(wn_sb[:, :], wnl[e:e + 1, :], waits=p.alltok())
            wtoks = []
            for mc in range(NMC):
                tg = p.dma(gst[:, :].rearrange("p (a c) -> p a c", c=128),
                           col_view(gp[e, :, :], mc * 128, 128),
                           waits=p.alltok())
                wtoks.append(p.v("tensor_copy",
                                 gbf[:, mc * 2048:(mc + 1) * 2048],
                                 gst[:, :], waits=[tg]))
                tu = p.dma(gst[:, :].rearrange("p (a c) -> p a c", c=128),
                           col_view(up[e, :, :], mc * 128, 128),
                           waits=p.alltok())
                wtoks.append(p.v("tensor_copy",
                                 ubf[:, mc * 2048:(mc + 1) * 2048],
                                 gst[:, :], waits=[tu]))
            for dc in range(NDC):
                td = p.dma(gst[:, 0:NMC * 128]
                           .rearrange("p (a c) -> p a c", c=128),
                           col_view(dp[e, :, :], dc * 128, 128),
                           waits=p.alltok())
                wtoks.append(p.v("tensor_copy",
                                 dbf[:, dc * ME:(dc + 1) * ME],
                                 gst[:, 0:NMC * 128], waits=[td]))
            twt = wtoks[-1]
            for ti in range(NTG):
                sl = ti % 2
                txn = p.dma(xnb[:, sl * XW:(sl + 1) * XW]
                            .rearrange("p (a c) -> p a c", c=TG),
                            xn1T.rearrange("(a p) t -> p a t", p=128)
                            [:, :, ti * TG:(ti + 1) * TG],
                            q="tq", waits=xcons[sl])
                tb = p.mm(pw[:, :], onesf[:, :],
                          wn_sb[:, ti * TG:(ti + 1) * TG], True, True,
                          waits=[twn])
                p.scopy(wnb[:, :], pw[:, :], waits=[tb])
                for mc in range(NMC):
                    for dc in range(NDC):
                        p.mm(pg[:, :], gbf[:, mc * 2048 + dc * 128:
                                           mc * 2048 + (dc + 1) * 128],
                             xnb[:, sl * XW + dc * TG:sl * XW + (dc + 1) * TG],
                             dc == 0, dc == NDC - 1, waits=[twt, txn])
                    for dc in range(NDC):
                        p.mm(pu[:, :], ubf[:, mc * 2048 + dc * 128:
                                           mc * 2048 + (dc + 1) * 128],
                             xnb[:, sl * XW + dc * TG:sl * XW + (dc + 1) * TG],
                             dc == 0, dc == NDC - 1)
                    tm = p.tok("tensor")
                    p.act(sg[:, :], pg[:, :], AF.Silu, waits=[tm])
                    p.v("tensor_mul", sg[:, :], sg[:, :], pu[:, :])
                    p.v("tensor_mul", hw[:, mc * TG:(mc + 1) * TG], sg[:, :],
                        wnb[:, :])
                xcons[sl] = [p.tok("tensor"), p.tok("vector")]
                for dc in range(NDC):
                    for mc in range(NMC):
                        p.mm(pd[:, :], dbf[:, dc * ME + mc * 128:
                                           dc * ME + (mc + 1) * 128],
                             hw[:, mc * TG:(mc + 1) * TG],
                             mc == 0, mc == NMC - 1)
                    tc = p.scopy(prow[:, :], pd[:, :],
                                 waits=[p.tok("tensor"), p.tok("gq")])
                    last.append(p.dma(
                        scr[e, dc * 128:(dc + 1) * 128,
                            ti * TG:(ti + 1) * TG],
                        prow[:, :], q="gq", waits=[tc]))
        tscr = last[-1]
        for dc in range(NDC):
            for ti in range(NTG):
                sl = ti % 2
                t0 = p.dma(arow[:, sl * TG:(sl + 1) * TG],
                           scr[0, dc * 128:(dc + 1) * 128,
                               ti * TG:(ti + 1) * TG],
                           waits=[tscr] + p.alltok())
                p.v("tensor_copy", prow[:, :], arow[:, sl * TG:(sl + 1) * TG],
                    waits=[t0, p.tok("gq")])
                for e in range(1, 4):
                    te = p.dma(arow[:, sl * TG:(sl + 1) * TG],
                               scr[e, dc * 128:(dc + 1) * 128,
                                   ti * TG:(ti + 1) * TG],
                               waits=[p.tok("vector")])
                    p.v("tensor_add", prow[:, :], prow[:, :],
                        arow[:, sl * TG:(sl + 1) * TG], waits=[te])
                last.append(p.dma(moe_part[dc * 128:(dc + 1) * 128,
                                           ti * TG:(ti + 1) * TG],
                                  prow[:, :], q="gq", waits=[p.tok("vector")]))
        p.final_wait(last[-2:], eng="scalar")
    return p.build()


def build_final():
    p = Prog("final")
    nc = p.nc
    x1r = p.din("x1r", [RT, D], F32)
    mp0 = p.din("mp0", [D, RT], F32)
    mp1 = p.din("mp1", [D, RT], F32)
    mep = p.din("mep", [8, D, RT], BF16)
    identf = p.din("identf", [128, 128], F32)
    yr = p.dout("yr", [RT, D], F32)
    NDC = D // 128
    with contextlib.ExitStack() as st:
        id_f = sbt(st, nc, "id_f", [128, 128], F32)
        acc = sbt(st, nc, "acc", [128, NDC * RT], F32)
        la = sbt(st, nc, "la", [128, RT], F32)
        lb = sbt(st, nc, "lb", [128, RT], BF16)
        x_sb = sbt(st, nc, "x_sb", [128, D], F32)
        yrow = sbt(st, nc, "yrow", [128, D], F32)
        tpf = pst(st, nc, "tpf", [128, 128], F32)
        tid = p.dma(id_f[:, :], identf[:, :])
        for dc in range(NDC):
            t0 = p.dma(la[:, :], mp0[dc * 128:(dc + 1) * 128, :],
                       waits=p.alltok())
            t1 = p.dma(lb[:, :], mep[0, dc * 128:(dc + 1) * 128, :],
                       q="tq", waits=p.alltok())
            p.v("tensor_copy", acc[:, dc * RT:(dc + 1) * RT], la[:, :],
                waits=[t0])
            p.v("tensor_add", acc[:, dc * RT:(dc + 1) * RT],
                acc[:, dc * RT:(dc + 1) * RT], lb[:, :], waits=[t1])
            t2 = p.dma(la[:, :], mp1[dc * 128:(dc + 1) * 128, :],
                       waits=[p.tok("vector")])
            p.v("tensor_add", acc[:, dc * RT:(dc + 1) * RT],
                acc[:, dc * RT:(dc + 1) * RT], la[:, :], waits=[t2])
            for e in range(1, 8):
                te = p.dma(lb[:, :], mep[e, dc * 128:(dc + 1) * 128, :],
                           q="tq", waits=[p.tok("vector")])
                p.v("tensor_add", acc[:, dc * RT:(dc + 1) * RT],
                    acc[:, dc * RT:(dc + 1) * RT], lb[:, :], waits=[te])
        last = []
        for tt in range(8):
            tlx = p.dma(x_sb[:, :], x1r[tt * 128:(tt + 1) * 128, :],
                        waits=p.alltok())
            for dc in range(NDC):
                ttr = p.tr(tpf[:, :],
                           acc[:, dc * RT + tt * 128:dc * RT + (tt + 1) * 128],
                           id_f[:, :], waits=[tid])
                p.scopy(yrow[:, dc * 128:(dc + 1) * 128], tpf[:, :],
                        waits=[ttr, p.tok("gq")])
            p.v("tensor_add", yrow[:, :], yrow[:, :], x_sb[:, :],
                waits=[tlx, p.tok("scalar"), p.tok("gq")])
            last.append(p.dma(yr[tt * 128:(tt + 1) * 128, :], yrow[:, :],
                              q="gq", waits=[p.tok("vector")]))
        p.final_wait(last[-1:], eng="scalar")
    return p.build()


def _run(name, builder, in_maps):
    nc = get_nc(name, builder)
    res = run_bass_kernel_spmd(nc, in_maps, core_ids=list(range(NCORES)))
    return res.results


def kernel(_dump=None, _stop_after=None, **inp):
    f32 = np.float32
    x = np.ascontiguousarray(np.asarray(inp["L0_attn_x"], f32).reshape(NT, D))
    rows = [x[c * RT:(c + 1) * RT] for c in range(NCORES)]

    r = _run("ln0", build_ln0,
             [{"xr": rows[c], "identb": IDENT_BF16} for c in range(NCORES)])
    xnT_parts = [r[c]["xnT_c"] for c in range(NCORES)]
    xnT_b = [np.ascontiguousarray(
        np.concatenate([xnT_parts[2 * b], xnT_parts[2 * b + 1]], axis=1))
        for b in range(B)]
    print("[kernel] stage ln0 done", flush=True)
    if _dump is not None:
        _dump["ln0"] = dict(xn_r=[r[c]['xn_r'] for c in range(NCORES)], xnT=xnT_parts)
    if _stop_after == "ln0":
        return None

    gi = np.asarray(inp["L0_ffn_gate_idx"], np.int32)
    ui = np.asarray(inp["L0_ffn_up_idx"], np.int32)
    dxi = np.asarray(inp["L0_ffn_down_idx"], np.int32)
    lut = np.asarray(inp["L0_ffn_lut"], f32)
    dec_maps = []
    for c in range(NCORES):
        gsl = np.concatenate(
            [gi[min(6 * c + j, OCN - 1) * 128:(min(6 * c + j, OCN - 1) + 1) * 128]
             for j in range(6)], axis=0)
        usl = np.concatenate(
            [ui[min(6 * c + j, OCN - 1) * 128:(min(6 * c + j, OCN - 1) + 1) * 128]
             for j in range(6)], axis=0)
        dsl = dxi[c * 256:(c + 1) * 256]
        dec_maps.append({"g_idx": np.ascontiguousarray(gsl),
                         "u_idx": np.ascontiguousarray(usl),
                         "d_idx": np.ascontiguousarray(dsl),
                         "lut": lut, "identf": IDENT_F32})
    r = _run("decode", build_decode, dec_maps)
    gT = np.concatenate([r[c]["gT_part"] for c in range(NCORES)], axis=1)[:, :INTER]
    uT = np.concatenate([r[c]["uT_part"] for c in range(NCORES)], axis=1)[:, :INTER]
    dT = np.concatenate([r[c]["dT_part"] for c in range(NCORES)], axis=1)
    gT = np.ascontiguousarray(gT)
    uT = np.ascontiguousarray(uT)
    dT = np.ascontiguousarray(dT)
    print("[kernel] stage decode done", flush=True)
    if _dump is not None:
        _dump["decode"] = dict(gT=gT, uT=uT, dT=dT)
    if _stop_after == "decode":
        return None

    bqkv = np.concatenate([np.asarray(inp["L0_attn_b_Q"], f32),
                           np.asarray(inp["L0_attn_b_K"], f32),
                           np.asarray(inp["L0_attn_b_V"], f32)], axis=1)
    att_maps = []
    for c in range(NCORES):
        b, qh = c // 2, c % 2
        mqk = np.broadcast_to(
            (np.arange(16, dtype=f32) * 128 - qh * RT)[None, :],
            (128, 16)).copy()
        att_maps.append({
            "xnT": xnT_b[b], "xnTq": xnT_parts[c],
            "wq": np.asarray(inp["L0_attn_W_Q"], f32),
            "wk": np.asarray(inp["L0_attn_W_K"], f32),
            "wv": np.asarray(inp["L0_attn_W_V"], f32),
            "wo": np.asarray(inp["L0_attn_W_O"], f32),
            "bqkv": np.ascontiguousarray(bqkv),
            "bo": np.asarray(inp["L0_attn_b_O"], f32),
            "mqk": mqk, "identf": IDENT_F32, "identb": IDENT_BF16})
    r = _run("attn0", build_attn0, att_maps)
    attn_rows = [r[c]["attn_r"] for c in range(NCORES)]
    print("[kernel] stage attn0 done", flush=True)
    if _dump is not None:
        _dump["attn0"] = dict(attn_rows=attn_rows)
    if _stop_after == "attn0":
        return None

    r = _run("ffn0", build_ffn0,
             [{"xnTq": xnT_parts[c], "gT": gT, "uT": uT, "dT": dT}
              for c in range(NCORES)])
    ffnT = [r[c]["ffnT"] for c in range(NCORES)]
    print("[kernel] stage ffn0 done", flush=True)
    if _dump is not None:
        _dump["ffn0"] = dict(ffnT=ffnT)
    if _stop_after == "ffn0":
        return None

    gk = np.asarray(inp["L1_ffn_gate_kernel"], f32)
    r = _run("resln1", build_resln1,
             [{"xr": rows[c], "ar": attn_rows[c], "ffnT": ffnT[c],
               "gk": gk, "identf": IDENT_F32, "identb": IDENT_BF16}
              for c in range(NCORES)])
    x1_rows = [r[c]["x1_r"] for c in range(NCORES)]
    xn1T_parts = [r[c]["xn1T_c"] for c in range(NCORES)]
    wnT_all = np.ascontiguousarray(
        np.concatenate([r[c]["wnT"] for c in range(NCORES)], axis=1))
    xn1T_b = [np.ascontiguousarray(
        np.concatenate([xn1T_parts[2 * b], xn1T_parts[2 * b + 1]], axis=1))
        for b in range(B)]
    xn1T_all = np.ascontiguousarray(np.concatenate(xn1T_b, axis=1))
    print("[kernel] stage resln1 done", flush=True)
    if _dump is not None:
        _dump["resln1"] = dict(x1_rows=x1_rows, xn1T=xn1T_parts, wnT=wnT_all)
    if _stop_after == "resln1":
        return None

    ipw = np.asarray(inp["L1_attn_in_proj_w"], f32)
    ipwTx = np.ascontiguousarray(ipw[:DI].T)
    cw = np.asarray(inp["L1_attn_conv_w"], f32)
    cb = np.asarray(inp["L1_attn_conv_b"], f32)
    xpwT = np.ascontiguousarray(np.asarray(inp["L1_attn_x_proj_w"], f32).T)
    m1_maps = []
    for c in range(NCORES):
        b, half = c // 2, c % 2
        ipwTz = np.ascontiguousarray(
            ipw[DI + half * 2048: DI + (half + 1) * 2048].T)
        m1_maps.append({
            "xn1T": xn1T_b[b], "ipwTx": ipwTx, "ipwTz": ipwTz,
            "xpwT": xpwT,
            "cw": np.ascontiguousarray(cw),
            "cb": np.ascontiguousarray(cb.reshape(32, 128)),
            "identf": IDENT_F32})
    r = _run("mamba1", build_mamba1, m1_maps)
    xc_l = [np.ascontiguousarray(
        r[c]["xc_all"][(c % 2) * 2048:((c % 2) + 1) * 2048])
        for c in range(NCORES)]
    z_l = [r[c]["z_l"] for c in range(NCORES)]
    dtlo = [r[c]["dtlo"] for c in range(NCORES)]
    BCo = [r[c]["BCo"] for c in range(NCORES)]
    print("[kernel] stage mamba1 done", flush=True)
    if _dump is not None:
        _dump["mamba1"] = dict(xc_l=xc_l, z_l=z_l, dtlo=dtlo, BCo=BCo)
    if _stop_after == "mamba1":
        return None

    dtw = np.asarray(inp["L1_attn_dt_proj_w"], f32)
    dtb = np.asarray(inp["L1_attn_dt_proj_b"], f32)
    alog = np.asarray(inp["L1_attn_A_log"], f32)
    dv = np.asarray(inp["L1_attn_D"], f32)
    opw = np.asarray(inp["L1_attn_out_proj_w"], f32)
    opwT = np.ascontiguousarray(opw.T)
    sc_maps = []
    for c in range(NCORES):
        half = c % 2
        lo, hi = half * 2048, (half + 1) * 2048
        sc_maps.append({
            "dtlo": dtlo[c], "BCi": BCo[c], "xc": xc_l[c], "zi": z_l[c],
            "dtwT": np.ascontiguousarray(dtw[lo:hi].T),
            "dtb": np.ascontiguousarray(dtb[lo:hi].reshape(16, 128)),
            "alog": np.ascontiguousarray(alog[lo:hi]),
            "dvec": np.ascontiguousarray(dv[lo:hi].reshape(16, 128)),
            "opwT": np.ascontiguousarray(opwT[lo:hi]),
            "identf": IDENT_F32})
    r = _run("scan", build_scan, sc_maps)
    mpart = [r[c]["mpart"] for c in range(NCORES)]
    print("[kernel] stage scan done", flush=True)
    if _dump is not None:
        _dump["scan"] = dict(mpart=mpart)
    if _stop_after == "scan":
        return None

    gp = np.asarray(inp["L1_ffn_gate_proj"], f32)
    upj = np.asarray(inp["L1_ffn_up_proj"], f32)
    dpj = np.asarray(inp["L1_ffn_down_proj"], f32)
    moe_maps = []
    for c in range(NCORES):
        moe_maps.append({
            "xn1T": xn1T_all,
            "gp": np.ascontiguousarray(gp[4 * c:4 * c + 4]),
            "up": np.ascontiguousarray(upj[4 * c:4 * c + 4]),
            "dp": np.ascontiguousarray(dpj[4 * c:4 * c + 4]),
            "wnl": np.ascontiguousarray(wnT_all[4 * c:4 * c + 4])})
    r = _run("moe", build_moe, moe_maps)
    moe_parts = [r[c]["moe_part"] for c in range(NCORES)]
    print("[kernel] stage moe done", flush=True)
    if _dump is not None:
        _dump["moe"] = dict(moe_parts=moe_parts)
    if _stop_after == "moe":
        return None

    fin_maps = []
    for c in range(NCORES):
        b, qh = c // 2, c % 2
        sl = slice(qh * RT, (qh + 1) * RT)
        mep = np.stack([moe_parts[j][:, c * RT:(c + 1) * RT]
                        for j in range(NCORES)], axis=0)
        fin_maps.append({
            "x1r": x1_rows[c],
            "mp0": np.ascontiguousarray(mpart[2 * b][:, sl]),
            "mp1": np.ascontiguousarray(mpart[2 * b + 1][:, sl]),
            "mep": np.ascontiguousarray(mep),
            "identf": IDENT_F32})
    r = _run("final", build_final, fin_maps)
    out = np.concatenate([r[c]["yr"] for c in range(NCORES)], axis=0)
    return np.ascontiguousarray(out.reshape(B, S, D)).astype(np.float32)
